# version 04
# baseline (speedup 1.0000x reference)
import numpy as np
import ml_dtypes
from contextlib import ExitStack

import concourse.bass as bass
import concourse.mybir as mybir
from concourse.alu_op_type import AluOpType as OP
from concourse.bass_utils import run_bass_kernel_spmd

F32 = mybir.dt.float32
BF16 = mybir.dt.bfloat16
AF = mybir.ActivationFunctionType
U32 = mybir.dt.uint32

NCORES = 8
L, D, NH, HD, E = 2048, 1024, 16, 64, 8
SEQ = 4
NTOK = SEQ * L
NTILE = NTOK // 128
TPS = L // 128
NBLK = 8
TPB = NTILE // NBLK
CAP = float(NCORES * NTOK // E)
BISECT_ITERS = 34

BUILD_PHASES = 3
ATTN_PART = 3
DEBUG_DUMPS = False


class S:
    def __init__(self, nc, stack, name):
        self.h = stack.enter_context(nc.semaphore(name))
        self.n = 0

    def bump(self, instr, kk=1):
        instr.then_inc(self.h, kk)
        self.n += kk
        return self.n

    def dma(self, instr):
        instr.then_inc(self.h, 16)
        self.n += 16
        return self.n


class Q:
    def __init__(self, eng):
        self.e = eng
        self.seen = {}

    def wait(self, sem, val):
        if sem is None or val is None or val <= 0:
            return
        if self.seen.get(id(sem), 0) >= val:
            return
        self.e.wait_ge(sem.h, val)
        self.seen[id(sem)] = val


class K:
    pass


def sl(i, n=128):
    return slice(i * n, (i + 1) * n)


def vdr(k):
    return k.sDV.bump(k.nc.vector.drain(), 1)


def phase_barrier(k):
    nc = k.nc
    k.gp.wait(k.sST, k.sST.n)
    k.gp.wait(k.sOUT, k.sOUT.n)
    k.gp.wait(k.sG, k.sG.n)
    nc.all_engine_barrier()
    for q in (k.te, k.ve, k.se, k.gp, k.sy):
        q.seen = {}


def emit_ln(k, x_ap, out_ap):
    nc = k.nc
    dump, tsum, tsq, m, var, mm2 = k.ln_tmp
    nc.scalar.activation(dump[:, :], x_ap, AF.Square, accum_out=tsq[:, :])
    sc1 = k.sSC.bump(nc.scalar.activation(dump[:, :], x_ap, AF.Identity,
                                          accum_out=tsum[:, :]), 1)
    k.ve.wait(k.sSC, sc1)
    nc.vector.tensor_scalar(out=m[:, :], in0=tsum[:, :], scalar1=1.0 / D,
                            scalar2=None, op0=OP.mult)
    nc.vector.tensor_scalar(out=var[:, :], in0=tsq[:, :], scalar1=1.0 / D,
                            scalar2=1e-5, op0=OP.mult, op1=OP.add)
    nc.vector.scalar_tensor_tensor(out=mm2[:, :], in0=m[:, :], scalar=-1.0,
                                   in1=m[:, :], op0=OP.mult, op1=OP.mult)
    nc.vector.tensor_tensor(var[:, :], var[:, :], mm2[:, :], op=OP.add)
    nc.vector.tensor_scalar(out=var[:, :], in0=var[:, :], scalar1=1e-6,
                            scalar2=None, op0=OP.max)
    dv_var = vdr(k)
    k.se.wait(k.sDV, dv_var)
    nc.scalar.activation(var[:, :], var[:, :], AF.Ln)
    sc2 = k.sSC.bump(nc.scalar.activation(var[:, :], var[:, :], AF.Exp, scale=-0.5), 1)
    k.ve.wait(k.sSC, sc2)
    dv = k.sDV.bump(nc.vector.tensor_scalar(out=out_ap, in0=x_ap, scalar1=m[:, :],
                                            scalar2=var[:, :], op0=OP.subtract,
                                            op1=OP.mult), 1)
    return dv


def emit_ffn(k, call):
    nc = k.nc
    st = ExitStack()
    sb = lambda n, shp, dt: st.enter_context(nc.sbuf_tensor(f"{n}{call}", shp, dt))
    ps = lambda n, shp, dt=F32: st.enter_context(nc.psum_tensor(f"{n}{call}", shp, dt))

    res_d = k.x_d if call == 1 else k.y2_d
    xf_sc = k.xf1_sc if call == 1 else k.xf2_sc
    if call == 2 or BUILD_PHASES == 1:
        out_d, out_sem = k.out_d, k.sOUT
    else:
        out_d, out_sem = k.y1_d, k.sST

    xst = sb("xst", [128, 2 * D], F32)
    xfb = sb("xfb", [128, 2 * D], BF16)
    dump = sb("dmp", [128, D], BF16)
    lnt = [sb(f"lnt{i}", [128, 1], F32) for i in range(5)]
    k.ln_tmp = (dump, lnt[0], lnt[1], lnt[2], lnt[3], lnt[4])
    lam = sb("lam", [128, E], F32)
    prb = sb("prb", [128, E], F32)
    cg = sb("cg", [128, E * E], F32)
    rnk = sb("rnk", [128, E], F32)
    csx = sb("csx", [128, E], F32)
    tny = sb("tny", [128, 4], F32)
    R_tm = sb("Rtm", [128, NTILE * E], F32)
    sp_tm = sb("sptm", [128, NTILE * E], F32)
    s_tm = sb("stm", [128, NTILE * E], F32)
    rcom = sb("rcom", [128, NTOK], F32)
    RT8 = rcom[0:E, :]
    Rb = rcom[:, 0:4096]
    cntp = sb("cntp", [128, 2], F32)
    cnt8 = sb("cnt8", [E, 1], F32)
    lo8 = sb("lo8", [E, 1], F32)
    hi8 = sb("hi8", [E, 1], F32)
    mid8 = sb("mid8", [E, 1], F32)
    g8 = sb("g8", [E, 1], U32)
    mbc = sb("mbc", [128, 1], F32)
    lrow = sb("lrow", [1, E], F32)
    lobc = sb("lobc", [128, E], F32)
    xfT = sb("xfT", [128, 2 * 8192], BF16)
    wT = sb("wT", [128, 2 * 16384], BF16)
    hsb = sb("hsb", [128, 8 * 512], BF16)
    yacc = sb("yacc", [128, TPB * D], F32)
    yst = sb("yst", [128, 2 * D], F32)

    ph = [ps(f"ph{i}", [128, 512]) for i in range(2)]
    py = [ps(f"py{i}", [128, 512]) for i in range(4)]
    pm = ps("pm", [128, 512])
    pmb = ps("pmb", [128, 256], BF16)

    ph_free = [0, 0]
    py_free = [0, 0, 0, 0]
    pm_free = 0
    pmb_free = 0

    xst_free = [0, 0]
    xfb_free = [0, 0]
    if call == 1:
        xf_store = []
        for i in range(NTILE):
            par = i % 2
            k.sy.wait(k.sDV, xst_free[par])
            ld = k.sX.dma(nc.sync.dma_start(out=xst[:, sl(par, D)], in_=res_d[sl(i), :]))
            k.se.wait(k.sX, ld)
            k.ve.wait(k.sX, ld)
            k.ve.wait(k.sST, xfb_free[par])
            k.se.wait(k.sST, xfb_free[par])
            dv = emit_ln(k, xst[:, sl(par, D)], xfb[:, sl(par, D)])
            xst_free[par] = dv
            k.gp.wait(k.sDV, dv)
            stv = k.sST.dma(nc.gpsimd.dma_start(out=xf_sc[sl(i), :],
                                                in_=xfb[:, sl(par, D)]))
            xfb_free[par] = stv
            xf_store.append(stv)
    else:
        xf_store = list(k.xf2_store)

    k.te.wait(k.sG, k.gate_ld)
    k.te.wait(k.sW, k.consts_w)
    gT_dv = 0
    for c in range(8):
        k.te.wait(k.sDV, pmb_free)
        pe = k.sPE.bump(nc.tensor.transpose(pmb[:, 0:E], k.gate_bf[:, sl(c)],
                                            k.identb_sb[0:E, 0:E]), 1)
        k.ve.wait(k.sPE, pe)
        gT_dv = k.sDV.bump(nc.vector.tensor_copy(k.gateT[:, c * E:(c + 1) * E],
                                                 pmb[:, 0:E]), 1)
        pmb_free = gT_dv

    lam_all = yst[:, 0:NTILE * E]
    prb_all = yst[:, NTILE * E:2 * NTILE * E]
    asn_all = yst[:, 2 * NTILE * E:3 * NTILE * E]
    red_all = yst[:, 3 * NTILE * E:4 * NTILE * E]
    m_all = sb("mall", [128, NTILE], F32)
    xfT_free = [0, 0]
    for blk in range(NBLK):
        par = blk % 2
        k.sy.wait(k.sST, xf_store[blk * TPB + TPB - 1])
        k.sy.wait(k.sPE, xfT_free[par])
        for c in range(8):
            k.sTR.dma(nc.sync.dma_start(
                out=xfT[:, par * 8192 + c * 1024: par * 8192 + (c + 1) * 1024],
                in_=xf_sc[blk * 1024:(blk + 1) * 1024, sl(c)], transpose=True))
        trv = k.sTR.n
        k.te.wait(k.sTR, trv)
        k.te.wait(k.sDV, gT_dv)
        for t in range(TPB):
            i = blk * TPB + t
            bank = ph[t % 2]
            k.te.wait(k.sSC, ph_free[t % 2])
            for c in range(8):
                mm = nc.tensor.matmul(
                    bank[:, 0:E],
                    xfT[:, par * 8192 + c * 1024 + t * 128:
                        par * 8192 + c * 1024 + (t + 1) * 128],
                    k.gateT[:, c * E:(c + 1) * E], start=(c == 0), stop=(c == 7))
            pe = k.sPE.bump(mm, 1)
            k.se.wait(k.sPE, pe)
            ph_free[t % 2] = k.sSC.bump(
                nc.scalar.copy(lam_all[:, i * E:(i + 1) * E], bank[:, 0:E]), 1)
        xfT_free[par] = k.sPE.n

    lam_v = lam_all.rearrange("p (i e) -> p i e", e=E)
    prb_v3 = prb_all.rearrange("p (i e) -> p i e", e=E)
    red_v = red_all.rearrange("p (i e) -> p i e", e=E)
    k.ve.wait(k.sSC, k.sSC.n)
    nc.vector.tensor_reduce(m_all[:, :], lam_v, axis=mybir.AxisListType.X, op=OP.max)
    vdr(k)
    m_bc = m_all[:, :].rearrange("p (i o) -> p i o", o=1).broadcast_to([128, NTILE, E])
    nc.vector.tensor_tensor(prb_v3, lam_v, m_bc, op=OP.subtract)
    dvsub = vdr(k)
    k.se.wait(k.sDV, dvsub)
    sc_exp = k.sSC.bump(nc.scalar.activation(prb_all, prb_all, AF.Exp), 1)
    k.ve.wait(k.sSC, sc_exp)
    nc.vector.tensor_reduce(m_all[:, :], prb_v3, axis=mybir.AxisListType.X, op=OP.add)
    vdr(k)
    nc.vector.reciprocal(m_all[:, :], m_all[:, :])
    vdr(k)
    nc.vector.tensor_tensor(prb_v3, prb_v3, m_bc, op=OP.mult)
    vdr(k)
    cgw_v = rcom[:, :].rearrange("p (i a b) -> p i a b", a=E, b=E)
    cgw_v = rcom[:, 0:NTILE * E * E].rearrange("p (i a b) -> p i a b", a=E, b=E)
    pb_w = prb_v3.rearrange("p i (o e) -> p i o e", o=1).broadcast_to([128, NTILE, E, E])
    pa_w = prb_v3.rearrange("p i (e o) -> p i e o", o=1).broadcast_to([128, NTILE, E, E])
    nc.vector.tensor_tensor(cgw_v, pb_w, pa_w, op=OP.is_gt)
    nc.vector.tensor_reduce(red_v, cgw_v, axis=mybir.AxisListType.X, op=OP.add)
    vdr(k)
    nc.vector.tensor_tensor(cgw_v, cgw_v, pb_w, op=OP.mult)
    nc.vector.tensor_tensor(s_tm[:, :].rearrange("p (i e) -> p i e", e=E),
                            prb_v3, red_v, op=OP.subtract)
    nc.vector.tensor_reduce(red_v, cgw_v, axis=mybir.AxisListType.X, op=OP.add)
    vdr(k)
    nc.vector.tensor_scalar(out=asn_all, in0=red_all, scalar1=0.9,
                            scalar2=None, op0=OP.is_lt)
    vdr(k)
    nc.vector.tensor_scalar(out=s_tm[:, :], in0=s_tm[:, :], scalar1=99.0,
                            scalar2=None, op0=OP.add)
    nc.vector.tensor_tensor(s_tm[:, :], s_tm[:, :], asn_all, op=OP.mult)
    nc.vector.tensor_scalar(out=R_tm[:, :], in0=s_tm[:, :], scalar1=-100.0,
                            scalar2=None, op0=OP.add)
    dv_route = k.sDV.bump(
        nc.vector.scalar_tensor_tensor(out=sp_tm[:, :], in0=prb_all, scalar=0.5,
                                       in1=asn_all, op0=OP.mult, op1=OP.mult), 1)
    routing_dv = [dv_route] * NTILE
    for i in range(NTILE):
        k.te.wait(k.sDV, routing_dv[i])
        k.te.wait(k.sDV, pm_free)
        pe = k.sPE.bump(nc.tensor.transpose(pm[0:E, 0:128],
                                            R_tm[:, i * E:(i + 1) * E],
                                            k.ident_sb[:, :]), 1)
        k.ve.wait(k.sPE, pe)
        pm_free = k.sDV.bump(nc.vector.tensor_copy(RT8[:, sl(i)], pm[0:E, 0:128]), 1)
    k.gp.wait(k.sDV, pm_free)
    ship = k.sST.dma(nc.gpsimd.dma_start(
        out=k.ccin.ap().rearrange("(h e) f -> e h f", h=2),
        in_=RT8.rearrange("e (h f) -> e h f", h=2)))
    k.gp.wait(k.sST, ship)
    cc = k.sCC.bump(nc.gpsimd.collective_compute(
        "AllGather", OP.bypass, replica_groups=[list(range(NCORES))],
        ins=[k.ccin.ap().opt()], outs=[k.ccout.ap().opt()]), 1)
    k.sy.wait(k.sCC, cc)
    cc_view = k.ccout.ap().rearrange("(c h e) f -> e (c h) f", c=NCORES, h=2)
    for e in range(E):
        k.sX.dma(nc.sync.dma_start(out=rcom[16 * e:16 * (e + 1), 0:4096],
                                   in_=cc_view[e]))
    rb_ld = k.sX.n
    k.ve.wait(k.sX, rb_ld)
    k.ve.wait(k.sST, k.sST.n)
    nc.vector.memset(lo8[:, :], -101.0)
    nc.vector.memset(hi8[:, :], 1.0)
    vdr(k)
    for it in range(BISECT_ITERS):
        nc.vector.tensor_tensor(mid8[:, :], lo8[:, :], hi8[:, :], op=OP.add)
        vdr(k)
        nc.vector.tensor_scalar(out=mid8[:, :], in0=mid8[:, :], scalar1=0.5,
                                scalar2=None, op0=OP.mult)
        dvm = vdr(k)
        k.te.wait(k.sDV, dvm)
        pe1 = k.sPE.bump(nc.tensor.matmul(pm[:, 0:1], k.sel8B_sb[:, :], mid8[:, :],
                                          start=True, stop=True), 1)
        k.ve.wait(k.sPE, pe1)
        nc.vector.tensor_copy(mbc[:, :], pm[:, 0:1])
        vdr(k)
        nc.vector.tensor_scalar(out=xfb[:, :], in0=Rb[:, 0:2048],
                                scalar1=mbc[:, :], scalar2=0.0, op0=OP.is_ge,
                                op1=OP.add, accum_out=cntp[:, 0:1])
        nc.vector.tensor_scalar(out=xfb[:, :], in0=Rb[:, 2048:4096],
                                scalar1=mbc[:, :], scalar2=0.0, op0=OP.is_ge,
                                op1=OP.add, accum_out=cntp[:, 1:2])
        vdr(k)
        nc.vector.tensor_tensor(cntp[:, 0:1], cntp[:, 0:1], cntp[:, 1:2], op=OP.add)
        dvc = vdr(k)
        k.te.wait(k.sDV, dvc)
        pe2 = k.sPE.bump(nc.tensor.matmul(pm[0:E, 4:5], k.sel8T_sb[:, :],
                                          cntp[:, 0:1], start=True, stop=True), 1)
        k.ve.wait(k.sPE, pe2)
        nc.vector.tensor_copy(cnt8[:, :], pm[0:E, 4:5])
        vdr(k)
        nc.vector.tensor_scalar(out=g8[:, :], in0=cnt8[:, :], scalar1=CAP,
                                scalar2=None, op0=OP.is_ge)
        vdr(k)
        nc.vector.select(lo8[:, :], g8[:, :], mid8[:, :], lo8[:, :])
        vdr(k)
        nc.vector.select(hi8[:, :], g8[:, :], hi8[:, :], mid8[:, :])
        vdr(k)
    dv_lo = vdr(k)
    k.te.wait(k.sDV, dv_lo)
    pe = k.sPE.bump(nc.tensor.transpose(pm[0:1, 8:8 + E], lo8[:, :],
                                        k.ident_sb[0:E, 0:E]), 1)
    k.ve.wait(k.sPE, pe)
    nc.vector.tensor_copy(lrow[:, :], pm[0:1, 8:8 + E])
    dv = vdr(k)
    k.te.wait(k.sDV, dv)
    pe = k.sPE.bump(nc.tensor.matmul(pm[:, 16:16 + E], k.onesr_sb[:, :], lrow[:, :],
                                     start=True, stop=True), 1)
    k.ve.wait(k.sPE, pe)
    nc.vector.tensor_copy(lobc[:, :], pm[:, 16:16 + E])
    vdr(k)
    R_v = R_tm[:, :].rearrange("p (i e) -> p i e", e=E)
    lob_v = lobc[:, :].rearrange("p (o e) -> p o e", o=1).broadcast_to([128, NTILE, E])
    s_v = s_tm[:, :].rearrange("p (i e) -> p i e", e=E)
    sp_v = sp_tm[:, :].rearrange("p (i e) -> p i e", e=E)
    nc.vector.tensor_tensor(s_v, R_v, lob_v, op=OP.is_ge)
    vdr(k)
    nc.vector.tensor_tensor(s_v, s_v, sp_v, op=OP.mult)
    dv_s = vdr(k)
    pm_free = dv_s
    wT_ready = [(None, 0), (None, 0)]
    wcat_written = [0] * E

    def emit_wload(slot, wait_pe):
        e = slot % E
        par = slot % 2
        k.sy.wait(k.sPE, wait_pe)
        if call == 1 and slot < E:
            k.sy.wait(k.sG, k.wbf_ready)
            for c in range(8):
                k.sTR.dma(nc.sync.dma_start(
                    out=wT[:, par * 16384 + c * 1024: par * 16384 + (c + 1) * 1024],
                    in_=k.w1bf[e][:, sl(c)], transpose=True))
            for f in range(8):
                k.sTR.dma(nc.sync.dma_start(
                    out=wT[:, par * 16384 + 8192 + f * 1024:
                        par * 16384 + 8192 + (f + 1) * 1024],
                    in_=k.w2bf[e][:, sl(f)], transpose=True))
            wT_ready[par] = (k.sTR, k.sTR.n)
            k.gp.wait(k.sTR, k.sTR.n)
            wcat_written[e] = k.sST.dma(nc.gpsimd.dma_start(
                out=k.wcat[e], in_=wT[:, par * 16384:(par + 1) * 16384]))
        else:
            if call == 1:
                k.sy.wait(k.sST, wcat_written[e])
            wT_ready[par] = (k.sW, k.sW.dma(nc.sync.dma_start(
                out=wT[:, par * 16384:(par + 1) * 16384], in_=k.wcat[e])))

    emit_wload(0, 0)
    emit_wload(1, 0)
    xfT_rdy = [0, 0]
    yst_free = [(None, 0), (None, 0)]
    for blk in range(NBLK):
        bpar = blk % 2
        k.sy.wait(k.sST, xf_store[blk * TPB + TPB - 1])
        k.sy.wait(k.sPE, xfT_free[bpar])
        for c in range(8):
            k.sTR.dma(nc.sync.dma_start(
                out=xfT[:, bpar * 8192 + c * 1024: bpar * 8192 + (c + 1) * 1024],
                in_=xf_sc[blk * 1024:(blk + 1) * 1024, sl(c)], transpose=True))
        xfT_rdy[bpar] = k.sTR.n
        xb = bpar * 8192
        for e in range(E):
            epar = e % 2
            sem_w, val_w = wT_ready[epar]
            k.te.wait(sem_w, val_w)
            k.te.wait(k.sTR, xfT_rdy[bpar])
            wb = epar * 16384
            for sub in range(2):
                for f in range(8):
                    hb = ph[f % 2]
                    k.te.wait(k.sSC, ph_free[f % 2])
                    for c in range(8):
                        mm = nc.tensor.matmul(
                            hb[:, :],
                            wT[:, wb + c * 1024 + f * 128: wb + c * 1024 + (f + 1) * 128],
                            xfT[:, xb + c * 1024 + sub * 512:
                                xb + c * 1024 + (sub + 1) * 512],
                            start=(c == 0), stop=(c == 7))
                    pe = k.sPE.bump(mm, 1)
                    k.se.wait(k.sPE, pe)
                    ph_free[f % 2] = k.sSC.bump(
                        nc.scalar.activation(hsb[:, f * 512:(f + 1) * 512],
                                             hb[:, :], AF.Gelu), 1)
                for t4 in range(4):
                    t = sub * 4 + t4
                    i_glob = blk * TPB + t
                    for half in range(2):
                        ybi = (t4 % 2) * 2 + half
                        yb = py[ybi]
                        k.te.wait(k.sDV, py_free[ybi])
                        k.te.wait(k.sSC, ph_free[0])
                        k.te.wait(k.sSC, ph_free[1])
                        for f in range(8):
                            mm = nc.tensor.matmul(
                                yb[:, :],
                                hsb[:, f * 512 + t4 * 128: f * 512 + (t4 + 1) * 128],
                                wT[:, wb + 8192 + f * 1024 + half * 512:
                                    wb + 8192 + f * 1024 + (half + 1) * 512],
                                start=(f == 0), stop=(f == 7))
                        pe = k.sPE.bump(mm, 1)
                        k.ve.wait(k.sPE, pe)
                        ya = yacc[:, t * D + half * 512: t * D + (half + 1) * 512]
                        scol = s_tm[:, i_glob * E + e: i_glob * E + e + 1]
                        if e == 0:
                            ins = nc.vector.tensor_scalar(out=ya, in0=yb[:, :],
                                                          scalar1=scol, scalar2=None,
                                                          op0=OP.mult)
                        else:
                            ins = nc.vector.scalar_tensor_tensor(
                                out=ya, in0=yb[:, :], scalar=scol, in1=ya,
                                op0=OP.mult, op1=OP.add)
                        py_free[ybi] = k.sDV.bump(ins, 1)
            nslot = blk * E + e + 2
            if nslot < NBLK * E:
                emit_wload(nslot, k.sPE.n)
        xfT_free[bpar] = k.sPE.n
        for t in range(TPB):
            par = t % 2
            i_glob = blk * TPB + t
            if t < 2:
                k.sy.wait(k.sDV, xst_free[par])
                ld = k.sX.dma(nc.sync.dma_start(out=xst[:, sl(par, D)],
                                                in_=res_d[sl(i_glob), :]))
                if t == 0:
                    res_ld = [0] * TPB
                res_ld[t] = ld
            k.ve.wait(k.sX, res_ld[t])
            k.ve.wait(*yst_free[par])
            k.ve.wait(k.sDV, py_free[0])
            k.ve.wait(k.sDV, py_free[1])
            k.ve.wait(k.sDV, py_free[2])
            k.ve.wait(k.sDV, py_free[3])
            dv = k.sDV.bump(nc.vector.tensor_tensor(
                out=yst[:, sl(par, D)], in0=yacc[:, t * D:(t + 1) * D],
                in1=xst[:, sl(par, D)], op=OP.add), 1)
            xst_free[par] = dv
            k.gp.wait(k.sDV, dv)
            stv = out_sem.dma(nc.gpsimd.dma_start(out=out_d[sl(i_glob), :],
                                                  in_=yst[:, sl(par, D)]))
            yst_free[par] = (out_sem, stv)
            if t + 2 < TPB:
                k.sy.wait(k.sDV, xst_free[par])
                res_ld[t + 2] = k.sX.dma(nc.sync.dma_start(
                    out=xst[:, sl(par, D)], in_=res_d[sl(i_glob + 2), :]))
    st.close()


def emit_attention(k):
    nc = k.nc
    st = ExitStack()
    sb = lambda n, shp, dt: st.enter_context(nc.sbuf_tensor(n, shp, dt))
    ps = lambda n, shp, dt=F32: st.enter_context(nc.psum_tensor(n, shp, dt))

    wsl = sb("wsl", [128, 2 * 8 * D], BF16)
    bqkvo = sb("bqkvo", [1, 4 * D], BF16)
    cos_sb = sb("cos_sb", [128, TPS * 32], F32)
    sin_sb = sb("sin_sb", [128, TPS * 32], F32)
    tri_sb = sb("tri_sb", [128, 4 * 512], BF16)
    xst = sb("a_xst", [128, 2 * D], F32)
    xfb = sb("a_xfb", [128, 2 * D], BF16)
    lnt = [sb(f"a_lnt{i}", [128, 1], F32) for i in range(5)]
    xfaT = sb("xfaT", [128, 2 * 8 * 512], BF16)
    qk_tm = sb("qk_tm", [128, D], F32)
    rrt = sb("rrt", [128, D], F32)
    rr = [rrt[:, 0:512], rrt[:, 512:1024]]
    qkb = sb("qkb", [128, D], BF16)
    v_sb = sb("v_sb", [128, TPS * NH * 65], BF16)
    QT = sb("QT", [128, 4 * L], BF16)
    KT = sb("KT", [128, 4 * L], BF16)
    pT = sb("pT", [128, L], BF16)
    le = sb("le", [1, 512], F32)
    lr = sb("lr", [1, 512], BF16)
    craw = sb("craw", [128, L], BF16)
    ctxN = sb("ctxN", [128, 8 * L], BF16)
    k.ln_tmp = (craw[:, 0:D], lnt[0], lnt[1], lnt[2], lnt[3], lnt[4])
    y2st = [qk_tm[:, :], rrt[:, :]]

    pctx = ps("pctx", [65, L])
    psT = [ps(f"psT{i}", [128, 512]) for i in range(2)]
    pp = [ps(f"pp{i}", [128, 512]) for i in range(2)]

    for j, b_d in enumerate([k.bq, k.bk, k.bv, k.bo]):
        k.sG.dma(nc.gpsimd.dma_start(out=bqkvo[:, j * D:(j + 1) * D], in_=b_d))
    aw_g0 = k.sG.n
    k.sW.dma(nc.sync.dma_start(
        out=cos_sb[:, :].rearrange("p (i j) -> p i j", j=32),
        in_=k.cos_c.rearrange("(i p) j -> p i j", p=128)))
    k.sW.dma(nc.sync.dma_start(
        out=sin_sb[:, :].rearrange("p (i j) -> p i j", j=32),
        in_=k.sin_c.rearrange("(i p) j -> p i j", p=128)))
    k.sW.dma(nc.sync.dma_start(out=tri_sb[:, :], in_=k.trimask))
    aw_s = k.sW.n

    wsl_ready = [(None, 0), (None, 0)]
    wsl_free = [0, 0]

    def load_w(which, slot):
        src_w = {0: k.wq, 1: k.wk, 2: k.wv, 3: k.wo}[which]
        k.sy.wait(k.sPE, wsl_free[slot])
        k.sG.dma(nc.gpsimd.dma_start(
            out=wsl[:, slot * 8 * D:(slot + 1) * 8 * D].rearrange(
                "p (c j) -> p c j", j=D),
            in_=src_w.rearrange("(c p) j -> p c j", p=128)))
        wsl_ready[slot] = (k.sG, k.sG.n)

    xst_free = [0, 0]
    xfb_free = [0, 0]
    xfaT_free = [0, 0]
    qkb_free = 0
    y2st_free = [(None, 0), (None, 0)]
    pp_free = [(None, 0), (None, 0)]
    psT_free = [(None, 0), (None, 0)]
    pctx_free = [(None, 0)]
    pT_free = [0, 0]
    craw_free = 0
    qtkt_free = 0
    prev_xfaT_tr = 0
    prev_qkT_tr = 0

    k.xf2_store = [0] * NTILE
    load_w(0, 0)
    load_w(1, 1)
    nc.vector.memset(v_sb[:, :], 1.0)
    vdr(k)
    for s in range(SEQ):
        k.ve.wait(k.sPE, qtkt_free)
        k.gp.wait(k.sTR, prev_xfaT_tr)
        k.gp.wait(k.sTR, prev_qkT_tr)
        xfa_store = []
        for i in range(TPS):
            par = i % 2
            gi = s * TPS + i
            k.sy.wait(k.sDV, xst_free[par])
            ld = k.sX.dma(nc.sync.dma_start(out=xst[:, sl(par, D)],
                                            in_=k.y1_d[sl(gi), :]))
            k.se.wait(k.sX, ld)
            k.ve.wait(k.sX, ld)
            k.ve.wait(k.sST, xfb_free[par])
            dv = emit_ln(k, xst[:, sl(par, D)], xfb[:, sl(par, D)])
            xst_free[par] = dv
            k.gp.wait(k.sDV, dv)
            stv = k.sST.dma(nc.gpsimd.dma_start(out=k.xfa_sc[sl(i), :],
                                                in_=xfb[:, sl(par, D)]))
            xfb_free[par] = stv
            xfa_store.append(stv)
        for pidx, (which, slot, sc_d) in enumerate(
                [(0, 0, k.q_sc), (1, 1, k.k_sc), (2, 0, None)]):
            if pidx == 2:
                load_w(2, 0)
            sem_w, val_w = wsl_ready[slot]
            wb = slot * 8 * D
            for g in range(4):
                gpar = g % 2
                k.sy.wait(k.sST, xfa_store[g * 4 + 3])
                k.sy.wait(k.sPE, xfaT_free[gpar])
                for c in range(8):
                    k.sTR.dma(nc.sync.dma_start(
                        out=xfaT[:, gpar * 4096 + c * 512: gpar * 4096 + (c + 1) * 512],
                        in_=k.xfa_sc[g * 512:(g + 1) * 512, sl(c)], transpose=True))
                trv = k.sTR.n
                prev_xfaT_tr = trv
                k.te.wait(k.sTR, trv)
                k.te.wait(sem_w, val_w)
                k.te.wait(k.sW, aw_s)
                k.te.wait(k.sG, aw_g0)
                for ti in range(4):
                    i = g * 4 + ti

                    def lhsT(c):
                        return xfaT[:, gpar * 4096 + c * 512 + ti * 128:
                                    gpar * 4096 + c * 512 + (ti + 1) * 128]

                    for half in range(2):
                        bank = pp[half]
                        k.te.wait(*pp_free[half])
                        nc.tensor.matmul(
                            bank[:, :], k.onesb_sb[:, :],
                            bqkvo[:, which * D + half * 512: which * D + (half + 1) * 512],
                            start=True, stop=False)
                        for c in range(8):
                            mm = nc.tensor.matmul(
                                bank[:, :], lhsT(c),
                                wsl[:, wb + c * D + half * 512: wb + c * D + (half + 1) * 512],
                                start=False, stop=(c == 7))
                        pe = k.sPE.bump(mm, 1)
                        k.ve.wait(k.sPE, pe)
                        if pidx == 2:
                            vv = v_sb[:, i * NH * 65:(i + 1) * NH * 65].rearrange(
                                "p (h x) -> p h x", x=65)[:, half * 8:(half + 1) * 8, 0:64]
                            ins = nc.vector.tensor_copy(
                                vv, bank[:, :].rearrange("p (h x) -> p h x", x=64))
                        else:
                            ins = nc.vector.tensor_copy(
                                qk_tm[:, half * 512:(half + 1) * 512], bank[:, :])
                        pp_free[half] = (k.sDV, k.sDV.bump(ins, 1))
                    if pidx == 2:
                        continue
                    fv = qk_tm[:, :].rearrange("p (h x) -> p h x", x=64)[:, :, 0:32]
                    sv = qk_tm[:, :].rearrange("p (h x) -> p h x", x=64)[:, :, 32:64]
                    cosv = cos_sb[:, i * 32:(i + 1) * 32].rearrange(
                        "p (o j) -> p o j", o=1).broadcast_to([128, NH, 32])
                    sinv = sin_sb[:, i * 32:(i + 1) * 32].rearrange(
                        "p (o j) -> p o j", o=1).broadcast_to([128, NH, 32])
                    r1v = rr[0].rearrange("p (h j) -> p h j", j=32)
                    r2v = rr[1].rearrange("p (h j) -> p h j", j=32)
                    qv = qkb[:, :].rearrange("p (h x) -> p h x", x=64)
                    k.ve.wait(k.sST, qkb_free)
                    nc.vector.tensor_tensor(r1v, fv, cosv, op=OP.mult)
                    nc.vector.tensor_tensor(r2v, sv, sinv, op=OP.mult)
                    vdr(k)
                    nc.vector.tensor_tensor(qv[:, :, 0:32], r1v, r2v, op=OP.subtract)
                    vdr(k)
                    nc.vector.tensor_tensor(r1v, sv, cosv, op=OP.mult)
                    nc.vector.tensor_tensor(r2v, fv, sinv, op=OP.mult)
                    vdr(k)
                    dv = k.sDV.bump(nc.vector.tensor_tensor(qv[:, :, 32:64], r1v, r2v,
                                                            op=OP.add), 1)
                    k.gp.wait(k.sDV, dv)
                    qkb_free = k.sST.dma(nc.gpsimd.dma_start(out=sc_d[sl(i), :],
                                                             in_=qkb[:, :]))
                xfaT_free[gpar] = k.sPE.n
            if pidx == 0 and s == 0:
                pass
        qk_done = k.sST.n
        v_dv = k.sDV.n

        pT_bufs = [pT[:, :], xfaT[:, 4096:4096 + L]]
        bank4 = [psT[0], psT[1], pp[0], pp[1]]
        b4_free = [psT_free[0], psT_free[1], pp_free[0], pp_free[1]]
        for h in range(NH):
            rnd = h // 8
            ch = (h // 2) % 4
            gch = h // 2
            rb = (h % 2) * 64
            if h % 8 == 0:
                k.sy.wait(k.sST, qk_done)
                k.sy.wait(k.sPE, qtkt_free if rnd == 0 else k.sPE.n)
                for c in range(4 * rnd, 4 * rnd + 4):
                    k.sTR.dma(nc.sync.dma_start(
                        out=QT[:, (c - 4 * rnd) * L:(c - 4 * rnd + 1) * L],
                        in_=k.q_sc[:, sl(c)], transpose=True))
                    k.sTR.dma(nc.sync.dma_start(
                        out=KT[:, (c - 4 * rnd) * L:(c - 4 * rnd + 1) * L],
                        in_=k.k_sc[:, sl(c)], transpose=True))
                qkT_rdy = k.sTR.n
                prev_qkT_tr = qkT_rdy
            k.te.wait(k.sTR, qkT_rdy)
            k.te.wait(k.sDV, v_dv)
            prev_pv = None
            for kt in range(TPS):
                tpar = kt % 2
                k.se.wait(k.sPE, pT_free[tpar])
                for qc in range(kt // 4, 4):
                    bank = bank4[qc]
                    k.te.wait(*b4_free[qc])
                    pe = k.sPE.bump(nc.tensor.matmul(
                        bank[:, :],
                        KT[rb:rb + 64, ch * L + kt * 128: ch * L + (kt + 1) * 128],
                        QT[rb:rb + 64, ch * L + qc * 512: ch * L + (qc + 1) * 512],
                        start=True, stop=True), 1)
                    k.se.wait(k.sPE, pe)
                    sc = k.sSC.bump(nc.scalar.activation(
                        pT_bufs[tpar][:, qc * 512:(qc + 1) * 512],
                        bank[:, :], AF.Exp, scale=0.125), 1)
                    b4_free[qc] = (k.sSC, sc)
                dqc = kt // 4
                k.ve.wait(k.sSC, k.sSC.n)
                dv_mask = k.sDV.bump(nc.vector.tensor_tensor(
                    pT_bufs[tpar][:, dqc * 512:(dqc + 1) * 512],
                    pT_bufs[tpar][:, dqc * 512:(dqc + 1) * 512],
                    tri_sb[:, (kt % 4) * 512:(kt % 4 + 1) * 512], op=OP.mult), 1)
                if prev_pv is not None:
                    pkt, ppar, pmask = prev_pv
                    k.te.wait(k.sDV, pmask)
                    if pkt == 0:
                        for _pf in (pctx_free if isinstance(pctx_free, list)
                                    else [pctx_free]):
                            k.te.wait(*_pf)
                    for qc in range(pkt // 4, 4):
                        mm = nc.tensor.matmul(
                            pctx[:, qc * 512:(qc + 1) * 512],
                            v_sb[:, pkt * NH * 65 + h * 65: pkt * NH * 65 + (h + 1) * 65],
                            pT_bufs[ppar][:, qc * 512:(qc + 1) * 512],
                            start=(pkt == 0), stop=(pkt == qc * 4 + 3))
                    pT_free[ppar] = k.sPE.bump(mm, 1)
                prev_pv = (kt, tpar, dv_mask)
            pkt, ppar, pmask = prev_pv
            k.te.wait(k.sDV, pmask)
            for qc in range(pkt // 4, 4):
                mm = nc.tensor.matmul(
                    pctx[:, qc * 512:(qc + 1) * 512],
                    v_sb[:, pkt * NH * 65 + h * 65: pkt * NH * 65 + (h + 1) * 65],
                    pT_bufs[ppar][:, qc * 512:(qc + 1) * 512],
                    start=(pkt == 0), stop=(pkt == qc * 4 + 3))
            pT_free[ppar] = k.sPE.bump(mm, 1)
            head_pe = k.sPE.n
            k.se.wait(k.sPE, head_pe)
            k.se.wait(k.sDV, craw_free)
            sc_c = k.sSC.bump(nc.scalar.copy(craw[0:64, :], pctx[0:64, :]), 1)
            lr_pe = 0
            for qc in range(4):
                pbi = qc % 2
                k.se.wait(k.sPE, lr_pe)
                nc.scalar.activation(le[:, :],
                                     pctx[64:65, qc * 512:(qc + 1) * 512], AF.Ln)
                dv_r = k.sSC.bump(
                    nc.scalar.activation(lr[:, :], le[:, :], AF.Exp, scale=-1.0), 1)
                k.te.wait(k.sSC, dv_r)
                k.te.wait(*b4_free[2 + pbi])
                k.te.wait(*pp_free[pbi])
                pe = k.sPE.bump(nc.tensor.matmul(
                    pp[pbi][0:64, :], k.onesb_sb[:, 0:64],
                    lr[:, :], start=True, stop=True), 1)
                lr_pe = pe
                k.ve.wait(k.sPE, pe)
                k.ve.wait(k.sSC, sc_c)
                craw_free = k.sDV.bump(nc.vector.tensor_tensor(
                    ctxN[rb:rb + 64, gch * L + qc * 512: gch * L + (qc + 1) * 512],
                    craw[0:64, qc * 512:(qc + 1) * 512],
                    pp[pbi][0:64, :], op=OP.mult), 1)
                pp_free[pbi] = (k.sDV, craw_free)
                b4_free[2 + pbi] = pp_free[pbi]
            pctx_free = [(k.sSC, sc_c), (k.sDV, craw_free)]
        ctx_dv = k.sDV.n
        psT_free[0], psT_free[1] = b4_free[0], b4_free[1]
        pp_free[0], pp_free[1] = b4_free[2], b4_free[3]
        wsl_free[1] = k.sPE.n
        load_w(3, 1)
        sem_w, val_w = wsl_ready[1]
        k.te.wait(sem_w, val_w)
        for i in range(TPS):
            par = i % 2
            gi = s * TPS + i
            k.sy.wait(k.sDV, xst_free[par])
            res_ld = k.sX.dma(nc.sync.dma_start(out=xst[:, sl(par, D)],
                                                in_=k.y1_d[sl(gi), :]))
            k.ve.wait(*y2st_free[par])
            for half in range(2):
                pbi = half
                k.te.wait(*pp_free[pbi])
                k.te.wait(k.sDV, ctx_dv)
                nc.tensor.matmul(pp[pbi][:, :], k.onesb_sb[:, :],
                                 bqkvo[:, 3 * D + half * 512: 3 * D + (half + 1) * 512],
                                 start=True, stop=False)
                for c in range(8):
                    mm = nc.tensor.matmul(
                        pp[pbi][:, :],
                        ctxN[:, c * L + i * 128: c * L + (i + 1) * 128],
                        wsl[:, 8 * D + c * D + half * 512: 8 * D + c * D + (half + 1) * 512],
                        start=False, stop=(c == 7))
                pe = k.sPE.bump(mm, 1)
                k.ve.wait(k.sPE, pe)
                k.ve.wait(k.sX, res_ld)
                dv = k.sDV.bump(nc.vector.tensor_tensor(
                    out=y2st[par][:, half * 512:(half + 1) * 512],
                    in0=pp[pbi][:, :],
                    in1=xst[:, par * D + half * 512: par * D + (half + 1) * 512],
                    op=OP.add), 1)
                pp_free[pbi] = (k.sDV, dv)
            y2_dv = k.sDV.n
            xst_free[par] = y2_dv
            if BUILD_PHASES >= 3:
                o2_d, o2_s = k.y2_d, k.sST
            else:
                o2_d, o2_s = k.out_d, k.sOUT
            k.gp.wait(k.sDV, y2_dv)
            stv = o2_s.dma(nc.gpsimd.dma_start(out=o2_d[sl(gi), :],
                                               in_=y2st[par][:, :]))
            y2st_free[par] = (o2_s, stv)
            if BUILD_PHASES >= 3:
                k.se.wait(k.sDV, y2_dv)
                k.ve.wait(k.sST, xfb_free[par])
                dv = emit_ln(k, y2st[par][:, :], xfb[:, sl(par, D)])
                k.gp.wait(k.sDV, dv)
                xfb_free[par] = k.sST.dma(nc.gpsimd.dma_start(
                    out=k.xf2_sc[sl(gi), :], in_=xfb[:, sl(par, D)]))
                k.xf2_store[gi] = xfb_free[par]
        qtkt_free = k.sPE.n
        xfaT_free[0] = k.sPE.n
        xfaT_free[1] = k.sPE.n
        if s + 1 < SEQ:
            wsl_free[0] = k.sPE.n
            wsl_free[1] = k.sPE.n
            load_w(0, 0)
            load_w(1, 1)
    st.close()


def build_program():
    nc = bass.Bass("TRN2", target_bir_lowering=False)
    k = K()
    k.nc = nc
    stack = ExitStack()

    k.x_d = nc.dram_tensor("x", [NTOK, D], F32, kind="ExternalInput").ap()
    k.wq = nc.dram_tensor("wq", [D, D], F32, kind="ExternalInput").ap()
    k.wk = nc.dram_tensor("wk", [D, D], F32, kind="ExternalInput").ap()
    k.wv = nc.dram_tensor("wv", [D, D], F32, kind="ExternalInput").ap()
    k.wo = nc.dram_tensor("wo", [D, D], F32, kind="ExternalInput").ap()
    k.bq = nc.dram_tensor("bq", [1, D], F32, kind="ExternalInput").ap()
    k.bk = nc.dram_tensor("bk", [1, D], F32, kind="ExternalInput").ap()
    k.bv = nc.dram_tensor("bv", [1, D], F32, kind="ExternalInput").ap()
    k.bo = nc.dram_tensor("bo", [1, D], F32, kind="ExternalInput").ap()
    k.gate = nc.dram_tensor("gate", [E, D], F32, kind="ExternalInput").ap()
    k.w1 = nc.dram_tensor("w1", [E, D, D], F32, kind="ExternalInput").ap()
    k.w2 = nc.dram_tensor("w2", [E, D, D], F32, kind="ExternalInput").ap()
    k.ident = nc.dram_tensor("ident", [128, 128], F32, kind="ExternalInput").ap()
    k.onesr = nc.dram_tensor("onesr", [1, 128], F32, kind="ExternalInput").ap()
    k.sel8T = nc.dram_tensor("sel8T", [128, E], F32, kind="ExternalInput").ap()
    k.sel8B = nc.dram_tensor("sel8B", [E, 128], F32, kind="ExternalInput").ap()
    k.cos_c = nc.dram_tensor("cos_c", [L, HD // 2], F32, kind="ExternalInput").ap()
    k.sin_c = nc.dram_tensor("sin_c", [L, HD // 2], F32, kind="ExternalInput").ap()
    k.trimask = nc.dram_tensor("trimask", [128, 4 * 512], BF16, kind="ExternalInput").ap()
    k.out_d = nc.dram_tensor("out", [NTOK, D], F32, kind="ExternalOutput").ap()
    k.xf1_sc = nc.dram_tensor("xf1_sc", [NTOK, D], BF16).ap()
    k.xf2_sc = nc.dram_tensor("xf2_sc", [NTOK, D], BF16).ap()
    k.xfa_sc = nc.dram_tensor("xfa_sc", [L, D], BF16).ap()
    k.q_sc = nc.dram_tensor("q_sc", [L, D], BF16).ap()
    k.k_sc = nc.dram_tensor("k_sc", [L, D], BF16).ap()
    k.y1_d = nc.dram_tensor("y1_d", [NTOK, D], F32).ap()
    k.y2_d = nc.dram_tensor("y2_d", [NTOK, D], F32).ap()
    k.w1bf = nc.dram_tensor("w1bf", [E, D, D], BF16).ap()
    k.w2bf = nc.dram_tensor("w2bf", [E, D, D], BF16).ap()
    k.wcat = nc.dram_tensor("wcat", [E, 128, 16384], BF16).ap()
    k.ccin = nc.dram_tensor("ccin", [2 * E, NTOK // 2], F32)
    k.ccout = nc.dram_tensor("ccout", [NCORES * 2 * E, NTOK // 2], F32)

    k.te, k.ve, k.se = Q(nc.tensor), Q(nc.vector), Q(nc.scalar)
    k.gp, k.sy = Q(nc.gpsimd), Q(nc.sync)
    for nm in ["sW", "sX", "sTR", "sST", "sOUT", "sG", "sCC", "sPE", "sDV", "sSC",
               "sBG", "sBS"]:
        setattr(k, nm, S(nc, stack, nm))

    k.ident_sb = stack.enter_context(nc.sbuf_tensor("ident_sb", [128, 128], F32))
    k.onesr_sb = stack.enter_context(nc.sbuf_tensor("onesr_sb", [1, 128], F32))
    k.onesb_sb = stack.enter_context(nc.sbuf_tensor("onesb_sb", [1, 128], BF16))
    k.sel8T_sb = stack.enter_context(nc.sbuf_tensor("sel8T_sb", [128, E], F32))
    k.sel8B_sb = stack.enter_context(nc.sbuf_tensor("sel8B_sb", [E, 128], F32))
    k.gate_bf = stack.enter_context(nc.sbuf_tensor("gate_bf", [E, D], BF16))
    k.identb_sb = stack.enter_context(nc.sbuf_tensor("identb_sb", [128, 128], BF16))
    k.gateT = stack.enter_context(nc.sbuf_tensor("gateT", [128, 8 * E], BF16))

    k.sW.dma(nc.sync.dma_start(out=k.ident_sb[:, :], in_=k.ident))
    k.sW.dma(nc.sync.dma_start(out=k.onesr_sb[:, :], in_=k.onesr))
    k.sG.dma(nc.gpsimd.dma_start(out=k.onesb_sb[:, :], in_=k.onesr))
    k.sW.dma(nc.sync.dma_start(out=k.sel8T_sb[:, :], in_=k.sel8T))
    k.sW.dma(nc.sync.dma_start(out=k.sel8B_sb[:, :], in_=k.sel8B))
    k.consts_w = k.sW.n
    k.sG.dma(nc.gpsimd.dma_start(out=k.identb_sb[:, :], in_=k.ident))
    k.gate_ld = k.sG.dma(nc.gpsimd.dma_start(out=k.gate_bf[:, :], in_=k.gate))
    for e in range(E):
        k.sG.dma(nc.gpsimd.dma_start(out=k.w1bf[e], in_=k.w1[e]))
        k.sG.dma(nc.gpsimd.dma_start(out=k.w2bf[e], in_=k.w2[e]))
    k.wbf_ready = k.sG.n

    emit_ffn(k, call=1)
    if BUILD_PHASES >= 2:
        phase_barrier(k)
        emit_attention(k)
    if BUILD_PHASES >= 3:
        phase_barrier(k)
        emit_ffn(k, call=2)

    k.gp.wait(k.sOUT, k.sOUT.n)
    k.sy.wait(k.sOUT, k.sOUT.n)
    stack.close()
    return nc


def kernel(L0_attn_x, L0_attn_wq_w, L0_attn_wq_b, L0_attn_wk_w, L0_attn_wk_b,
           L0_attn_wv_w, L0_attn_wv_b, L0_attn_wo_w, L0_attn_wo_b,
           L0_ffn_gate_w, L0_ffn_w1, L0_ffn_w2):
    x = np.ascontiguousarray(np.asarray(L0_attn_x, dtype=np.float32))
    Bb = x.shape[0]
    shards = x.reshape(NCORES, SEQ * L, D)

    ident = np.eye(128, dtype=np.float32)
    onesr = np.ones((1, 128), np.float32)
    sel8T = (np.arange(128)[:, None] // 16 == np.arange(E)[None, :]).astype(np.float32)
    sel8B = np.ascontiguousarray(sel8T.T)
    frac = 2.0 * np.arange(HD // 2) / HD
    tsc = 10000.0 ** frac
    ang = np.arange(L)[:, None] / tsc[None, :]
    cos_c = np.cos(ang).astype(np.float32)
    sin_c = np.sin(ang).astype(np.float32)
    tri = np.zeros((128, 4 * 512), np.float32)
    kp = np.arange(128)[:, None]
    qf = np.arange(512)[None, :]
    for m in range(4):
        tri[:, m * 512:(m + 1) * 512] = (qf >= m * 128 + kp)
    tri = tri.astype(ml_dtypes.bfloat16)

    nc = build_program()
    base = {
        "wq": np.ascontiguousarray(L0_attn_wq_w, np.float32),
        "wk": np.ascontiguousarray(L0_attn_wk_w, np.float32),
        "wv": np.ascontiguousarray(L0_attn_wv_w, np.float32),
        "wo": np.ascontiguousarray(L0_attn_wo_w, np.float32),
        "bq": np.ascontiguousarray(L0_attn_wq_b, np.float32).reshape(1, D),
        "bk": np.ascontiguousarray(L0_attn_wk_b, np.float32).reshape(1, D),
        "bv": np.ascontiguousarray(L0_attn_wv_b, np.float32).reshape(1, D),
        "bo": np.ascontiguousarray(L0_attn_wo_b, np.float32).reshape(1, D),
        "gate": np.ascontiguousarray(L0_ffn_gate_w, np.float32),
        "w1": np.ascontiguousarray(L0_ffn_w1, np.float32),
        "w2": np.ascontiguousarray(L0_ffn_w2, np.float32),
        "ident": ident, "onesr": onesr, "sel8T": sel8T, "sel8B": sel8B,
        "cos_c": cos_c, "sin_c": sin_c, "trimask": tri,
    }
    in_maps = [dict(base, x=np.ascontiguousarray(shards[c])) for c in range(NCORES)]
    res = run_bass_kernel_spmd(nc, in_maps, core_ids=list(range(NCORES)))
    out = np.stack([r["out"] for r in res.results], 0)
    return out.reshape(Bb, L, D)


if __name__ == "__main__":
    pass
